# version 03
# baseline (speedup 1.0000x reference)
import numpy as np
import ml_dtypes
from contextlib import ExitStack

import concourse.bass as bass
import concourse.mybir as mybir
from concourse.bass_utils import run_bass_kernel_spmd

F32 = mybir.dt.float32
F16 = mybir.dt.float16
BF16 = mybir.dt.bfloat16
U8 = mybir.dt.uint8
U16 = mybir.dt.uint16
I32 = mybir.dt.int32
AX = mybir.AxisListType
ALU = mybir.AluOpType
ACT = mybir.ActivationFunctionType

NCORES = 8
D = 4096
S = 2048
B = 4
MTOK = B * S
MC = MTOK // NCORES
HD = 128
NH = 32
NKV = 8
INTER = 11008
GS = 128

_BF = ml_dtypes.bfloat16



def perm_plane(n):
    idx = np.arange(n).reshape(n // GS, GS)
    return np.concatenate([idx[:, 0::2], idx[:, 1::2]], axis=1).reshape(-1)

PERM_D = perm_plane(D)
PERM_I = perm_plane(INTER)

def qweight_bytes(qw):
    r, n = qw.shape
    return np.ascontiguousarray(
        qw.view(np.uint8).reshape(r, n, 4).transpose(0, 2, 1).reshape(4 * r, n))

def zeros_bytes(z):
    g, nw = z.shape
    return np.ascontiguousarray(z.view(np.uint8).reshape(g, nw * 4))

def rope_tables(scale=1.0):
    inv = 1.0 / (10000.0 ** (np.arange(0, HD, 2, dtype=np.float64) / HD))
    t = np.arange(S, dtype=np.float64)
    fr = np.outer(t, inv)
    return ((np.cos(fr) * scale).astype(np.float32),
            (np.sin(fr) * scale).astype(np.float32))


class Ctr:
    def __init__(self):
        self.v = {}
    def inc(self, sem, n=1):
        self.v[sem] = self.v.get(sem, 0) + n
        return self.v[sem]
    def val(self, sem):
        return self.v.get(sem, 0)



def build_ln(d=D):
    nc = bass.Bass("TRN2", target_bir_lowering=False, debug=False)
    x = nc.dram_tensor("x", [MC, d], F32, kind="ExternalInput").ap()
    y = nc.dram_tensor("y", [MC, d], BF16, kind="ExternalOutput").ap()
    nt = MC // 128
    with ExitStack() as es:
        sb = lambda n, s, t: es.enter_context(nc.sbuf_tensor(n, s, t))
        xt = [sb(f"xt{i}", [128, d], F32) for i in range(2)]
        yt = [sb(f"yt{i}", [128, d], BF16) for i in range(2)]
        dump = sb("dump", [128, d], BF16)
        sumc = [sb(f"sum{i}", [128, 1], F32) for i in range(2)]
        sqc = [sb(f"sq{i}", [128, 1], F32) for i in range(2)]
        mc_ = [sb(f"m{i}", [128, 1], F32) for i in range(2)]
        msq = [sb(f"msq{i}", [128, 1], F32) for i in range(2)]
        var = [sb(f"var{i}", [128, 1], F32) for i in range(2)]
        sd = [sb(f"sd{i}", [128, 1], F32) for i in range(2)]
        rstd = [sb(f"rstd{i}", [128, 1], F32) for i in range(2)]
        nb = [sb(f"nb{i}", [128, 1], F32) for i in range(2)]
        epsc = sb("epsc", [128, 1], F32)
        ld = [es.enter_context(nc.semaphore(f"ld{i}")) for i in range(2)]
        sv = es.enter_context(nc.semaphore("sv"))
        sa = es.enter_context(nc.semaphore("sa"))
        so = es.enter_context(nc.semaphore("so"))
        st = [es.enter_context(nc.semaphore(f"st{i}")) for i in range(2)]
        block = es.enter_context(nc.Block())
        C = Ctr()

        @block.sync
        def _(sync):
            for i in range(nt):
                b = i % 2
                if i >= 2:
                    sync.wait_ge(so, i - 1)
                sync.dma_start(out=xt[b][:, :], in_=x[128 * i:128 * (i + 1), :]).then_inc(ld[b], 16)
            sync.wait_ge(st[0], 16 * ((nt + 1) // 2))
            sync.wait_ge(st[1], 16 * (nt // 2))

        @block.vector
        def _(vector):
            vector.memset(epsc[:, :], 1e-5)
            for i in range(nt):
                b = i % 2
                vector.wait_ge(ld[b], 16 * (i // 2 + 1))
                vector.reduce_sum(sumc[b][:, :], xt[b][:, :], axis=AX.X).then_inc(sv, 1); C.inc(sv)
                vector.wait_ge(sv, C.val(sv))
                vector.tensor_scalar(mc_[b][:, :], sumc[b][:, :], 1.0 / d, None, ALU.mult).then_inc(sv, 1); C.inc(sv)
                vector.wait_ge(sv, C.val(sv))
                vector.tensor_tensor(msq[b][:, :], mc_[b][:, :], mc_[b][:, :], ALU.mult).then_inc(sv, 1); C.inc(sv)
                vector.wait_ge(sa, 2 * i + 1)
                vector.wait_ge(sv, C.val(sv))
                vector.tensor_scalar(var[b][:, :], sqc[b][:, :], 1.0 / d, msq[b][:, 0:1],
                                     ALU.mult, ALU.subtract).then_inc(sv, 1); C.inc(sv)
                vector.wait_ge(sa, 2 * i + 2)
                vector.reciprocal(rstd[b][:, :], sd[b][:, :]).then_inc(sv, 1); C.inc(sv)
                vector.wait_ge(sv, C.val(sv))
                vector.tensor_scalar(nb[b][:, :], mc_[b][:, :], rstd[b][:, 0:1], -1.0,
                                     ALU.mult, ALU.mult).then_inc(sv, 1); C.inc(sv)
                vector.wait_ge(sv, C.val(sv))
                if i >= 2:
                    vector.wait_ge(st[b], 16 * (i // 2))
                vector.tensor_scalar(yt[b][:, :], xt[b][:, :], rstd[b][:, 0:1], nb[b][:, 0:1],
                                     ALU.mult, ALU.add).then_inc(so, 1)

        @block.scalar
        def _(scalar):
            for i in range(nt):
                b = i % 2
                scalar.wait_ge(ld[b], 16 * (i // 2 + 1))
                scalar.activation(dump[:, :], xt[b][:, :], ACT.Square,
                                  accum_out=sqc[b][:, :]).then_inc(sa, 1)
                scalar.wait_ge(sv, 6 * i + 4)
                scalar.activation(sd[b][:, :], var[b][:, :], ACT.Sqrt, bias=epsc[:, 0:1]).then_inc(sa, 1)
                scalar.wait_ge(so, i + 1)
                scalar.dma_start(out=y[128 * i:128 * (i + 1), :], in_=yt[b][:, :]).then_inc(st[b], 16)
    return nc



def build_gemm(K, N, out_dt=F32, w_dt=F32):
    assert K % 128 == 0 and N % 128 == 0
    nk = K // 128
    nn = N // 128
    nm = MC // 512
    nc = bass.Bass("TRN2", target_bir_lowering=False, debug=False)
    xtd = nc.dram_tensor("xt", [K, MC], BF16, kind="ExternalInput").ap()
    wd = nc.dram_tensor("w", [K, N], w_dt, kind="ExternalInput").ap()
    yd = nc.dram_tensor("y", [N, MC], out_dt, kind="ExternalOutput").ap()
    wr = wd.rearrange("(kc kp) n -> kp kc n", kp=128)
    with ExitStack() as es:
        sb = lambda n, s, t: es.enter_context(nc.sbuf_tensor(n, s, t))
        ps = lambda n, s, t: es.enter_context(nc.psum_tensor(n, s, t))
        xts = sb("xts", [128, nk * MC], BF16)
        xv = [xts[:, MC * k:MC * (k + 1)] for k in range(nk)]
        wf = [sb(f"wf{i}", [128, nk * 128], w_dt) for i in range(2)]
        wb = [sb(f"wb{i}", [128, nk * 128], BF16) for i in range(2)] if w_dt != BF16 else None
        yo = [sb(f"yo{i}", [128, MC], out_dt) for i in range(2)]
        pp = [ps(f"pp{i}", [128, 512], F32) for i in range(4)]
        ld = [es.enter_context(nc.semaphore(f"ld{i}")) for i in range(2)]
        lx = es.enter_context(nc.semaphore("lx"))
        cv = es.enter_context(nc.semaphore("cv"))
        mm = es.enter_context(nc.semaphore("mm"))
        ev = es.enter_context(nc.semaphore("ev"))
        st = [es.enter_context(nc.semaphore(f"st{i}")) for i in range(2)]
        block = es.enter_context(nc.Block())

        @block.sync
        def _(sync):
            xr = xtd.rearrange("(kc kp) m -> kp kc m", kp=128)
            xo = xts.ap().rearrange("p (kc m) -> p kc m", kc=nk)
            for kq in range(0, nk, 8):
                kqe = min(kq + 8, nk)
                sync.dma_start(out=xo[:, kq:kqe, :], in_=xr[:, kq:kqe, :]).then_inc(lx, 16)
            for j in range(nn):
                b = j % 2
                if j >= 2:
                    sync.wait_ge(cv, j - 1)
                wo = wf[b].ap().rearrange("p (kc n) -> p kc n", kc=nk)
                for kq in range(0, nk, 8):
                    kqe = min(kq + 8, nk)
                    sync.dma_start(out=wo[:, kq:kqe, :],
                                   in_=wr[:, kq:kqe, 128 * j:128 * (j + 1)]).then_inc(ld[b], 16)
            sync.wait_ge(st[0], 16 * ((nn + 1) // 2))
            sync.wait_ge(st[1], 16 * (nn // 2))

        @block.vector
        def _(vector):
            for j in range(nn):
                b = j % 2
                if w_dt == BF16:
                    vector.wait_ge(mm, 2 * (j + 1))
                    vector.nop().then_inc(cv, 1)
                else:
                    vector.wait_ge(ld[b], 16 * ((nk + 7) // 8) * (j // 2 + 1))
                    if j >= 2:
                        vector.wait_ge(mm, 2 * (j - 1))
                    vector.tensor_copy(wb[b][:, :], wf[b][:, :]).then_inc(cv, 1)

        @block.tensor
        def _(tensor):
            tensor.wait_ge(lx, 16 * ((nk + 7) // 8))
            for j in range(nn):
                b = j % 2
                if w_dt == BF16:
                    tensor.wait_ge(ld[b], 16 * ((nk + 7) // 8) * (j // 2 + 1))
                else:
                    tensor.wait_ge(cv, j + 1)
                for m in range(nm):
                    pb = pp[2 * (j % 2) + m]
                    if j >= 2:
                        tensor.wait_ge(ev, 2 * (j - 1))
                    wsrc = wf[b] if w_dt == BF16 else wb[b]
                    for k in range(nk):
                        inst = tensor.matmul(pb[:, :], wsrc[:, 128 * k:128 * (k + 1)],
                                             xv[k][:, 512 * m:512 * (m + 1)],
                                             start=(k == 0), stop=(k == nk - 1))
                    inst.then_inc(mm, 1)

        @block.scalar
        def _(scalar):
            for j in range(nn):
                b = j % 2
                if j >= 2:
                    scalar.wait_ge(st[b], 16 * (j // 2))
                for m in range(nm):
                    scalar.wait_ge(mm, 2 * j + m + 1)
                    scalar.copy(yo[b][:, 512 * m:512 * (m + 1)],
                                pp[2 * (j % 2) + m][:, :]).then_inc(ev, 1)
                scalar.wait_ge(ev, 2 * (j + 1))
                scalar.dma_start(out=yd[128 * j:128 * (j + 1), :], in_=yo[b][:, :]).then_inc(st[b], 16)
    return nc



def build_rope():
    nc = bass.Bass("TRN2", target_bir_lowering=False, debug=False)
    q = nc.dram_tensor("q", [MC, NH * HD], F32, kind="ExternalInput").ap()
    k = nc.dram_tensor("k", [MC, NKV * HD], F32, kind="ExternalInput").ap()
    v = nc.dram_tensor("v", [MC, NKV * HD], F32, kind="ExternalInput").ap()
    cq = nc.dram_tensor("cosq", [MC, 64], F32, kind="ExternalInput").ap()
    sq = nc.dram_tensor("sinq", [MC, 64], F32, kind="ExternalInput").ap()
    ck = nc.dram_tensor("cosk", [MC, 64], F32, kind="ExternalInput").ap()
    sk = nc.dram_tensor("sink", [MC, 64], F32, kind="ExternalInput").ap()
    qo = nc.dram_tensor("qo", [MC, NH * HD], BF16, kind="ExternalOutput").ap()
    ko = nc.dram_tensor("ko", [MC, NKV * HD], BF16, kind="ExternalOutput").ap()
    vo = nc.dram_tensor("vo", [MC, NKV * HD], BF16, kind="ExternalOutput").ap()
    nt = MC // 128
    with ExitStack() as es:
        sb = lambda n, s, t: es.enter_context(nc.sbuf_tensor(n, s, t))
        qt = [sb(f"qt{i}", [128, NH * HD], F32) for i in range(2)]
        kt = [sb(f"kt{i}", [128, NKV * HD], F32) for i in range(2)]
        vt = [sb(f"vt{i}", [128, NKV * HD], F32) for i in range(2)]
        cqt = [sb(f"cqt{i}", [128, 64], F32) for i in range(2)]
        sqt = [sb(f"sqt{i}", [128, 64], F32) for i in range(2)]
        ckt = [sb(f"ckt{i}", [128, 64], F32) for i in range(2)]
        skt = [sb(f"skt{i}", [128, 64], F32) for i in range(2)]
        pa = sb("pa", [128, 256], F32)
        qot = [sb(f"qot{i}", [128, NH * HD], BF16) for i in range(2)]
        kot = [sb(f"kot{i}", [128, NKV * HD], BF16) for i in range(2)]
        vot = [sb(f"vot{i}", [128, NKV * HD], BF16) for i in range(2)]
        ld = [es.enter_context(nc.semaphore(f"ld{i}")) for i in range(2)]
        sv = es.enter_context(nc.semaphore("sv"))
        so = es.enter_context(nc.semaphore("so"))
        st = [es.enter_context(nc.semaphore(f"st{i}")) for i in range(2)]
        block = es.enter_context(nc.Block())
        C = Ctr()

        @block.sync
        def _(sync):
            for i in range(nt):
                b = i % 2
                if i >= 2:
                    sync.wait_ge(so, i - 1)
                for (dst, src) in ((qt[b], q), (kt[b], k), (vt[b], v)):
                    sync.dma_start(out=dst[:, :], in_=src[128 * i:128 * (i + 1), :]).then_inc(ld[b], 16)
                for (dst, src) in ((cqt[b], cq), (sqt[b], sq), (ckt[b], ck), (skt[b], sk)):
                    sync.dma_start(out=dst[:, :], in_=src[128 * i:128 * (i + 1), :]).then_inc(ld[b], 16)
            sync.wait_ge(st[0], 48 * ((nt + 1) // 2))
            sync.wait_ge(st[1], 48 * (nt // 2))

        @block.vector
        def _(vector):
            for i in range(nt):
                b = i % 2
                vector.wait_ge(ld[b], 112 * (i // 2 + 1))
                if i >= 2:
                    vector.wait_ge(st[b], 48 * (i // 2))
                for (src, dst, nh, ct, stt) in ((qt[b], qot[b], NH, cqt[b], sqt[b]),
                                                (kt[b], kot[b], NKV, ckt[b], skt[b])):
                    for h in range(nh):
                        lo = src[:, h * HD:h * HD + 64]
                        hi = src[:, h * HD + 64:h * HD + 128]
                        vector.tensor_tensor(pa[:, 0:64], lo, ct[:, :], ALU.mult)
                        vector.tensor_tensor(pa[:, 64:128], hi, stt[:, :], ALU.mult)
                        vector.tensor_tensor(pa[:, 128:192], hi, ct[:, :], ALU.mult)
                        vector.tensor_tensor(pa[:, 192:256], lo, stt[:, :], ALU.mult).then_inc(sv, 1); C.inc(sv)
                        vector.wait_ge(sv, C.val(sv))
                        vector.tensor_tensor(dst[:, h * HD:h * HD + 64], pa[:, 0:64], pa[:, 64:128], ALU.subtract)
                        vector.tensor_tensor(dst[:, h * HD + 64:h * HD + 128], pa[:, 128:192], pa[:, 192:256], ALU.add).then_inc(sv, 1); C.inc(sv)
                        vector.wait_ge(sv, C.val(sv))
                vector.tensor_copy(vot[b][:, :], vt[b][:, :]).then_inc(so, 1)

        @block.scalar
        def _(scalar):
            for i in range(nt):
                b = i % 2
                scalar.wait_ge(so, i + 1)
                scalar.dma_start(out=qo[128 * i:128 * (i + 1), :], in_=qot[b][:, :]).then_inc(st[b], 16)
                scalar.dma_start(out=ko[128 * i:128 * (i + 1), :], in_=kot[b][:, :]).then_inc(st[b], 16)
                scalar.dma_start(out=vo[128 * i:128 * (i + 1), :], in_=vot[b][:, :]).then_inc(st[b], 16)
    return nc



_NC_CACHE = {}

def get_nc(key, builder, *args, **kw):
    if key not in _NC_CACHE:
        _NC_CACHE[key] = builder(*args, **kw)
    return _NC_CACHE[key]


def run(nc, in_maps):
    res = run_bass_kernel_spmd(nc, in_maps, core_ids=list(range(NCORES)))
    return res.results


def launch_ln(x_tok):
    nc = get_nc("ln", build_ln)
    ins = [{"x": np.ascontiguousarray(x_tok[MC * c:MC * (c + 1)])} for c in range(NCORES)]
    outs = run(nc, ins)
    return np.concatenate([o["y"] for o in outs], axis=0)


def launch_gemm(xhatT, w, out_dt=F32):
    K, N = w.shape
    w_dt = BF16 if w.dtype == _BF else F32
    nc = get_nc(("gemm", K, N, str(out_dt), str(w_dt)), build_gemm, K, N, out_dt, w_dt)
    w = np.ascontiguousarray(w)
    ins = [{"xt": np.ascontiguousarray(xhatT[:, MC * c:MC * (c + 1)]), "w": w}
           for c in range(NCORES)]
    outs = run(nc, ins)
    return np.concatenate([o["y"] for o in outs], axis=1)


def launch_rope(q, k, v, cosq, sinq, cosk, sink):
    nc = get_nc("rope", build_rope)
    ins = []
    for c in range(NCORES):
        sl = slice(MC * c, MC * (c + 1))
        pos = (np.arange(MTOK) % S)[sl]
        ins.append({
            "q": np.ascontiguousarray(q[sl]), "k": np.ascontiguousarray(k[sl]),
            "v": np.ascontiguousarray(v[sl]),
            "cosq": np.ascontiguousarray(cosq[pos]), "sinq": np.ascontiguousarray(sinq[pos]),
            "cosk": np.ascontiguousarray(cosk[pos]), "sink": np.ascontiguousarray(sink[pos]),
        })
    outs = run(nc, ins)
    return (np.concatenate([o["qo"] for o in outs], 0),
            np.concatenate([o["ko"] for o in outs], 0),
            np.concatenate([o["vo"] for o in outs], 0))



def build_attn0():
    NHC = 16
    NKC = 4
    NQB = S // 128
    nc = bass.Bass("TRN2", target_bir_lowering=False, debug=False)
    qd = nc.dram_tensor("qf", [NHC * HD, S], BF16, kind="ExternalInput").ap()
    kd = nc.dram_tensor("kf", [NKC * HD, S], BF16, kind="ExternalInput").ap()
    vd = nc.dram_tensor("vt", [S, NKC * HD], BF16, kind="ExternalInput").ap()
    idd = nc.dram_tensor("ident", [128, 128], BF16, kind="ExternalInput").ap()
    mdd = nc.dram_tensor("madd", [128, 128], F32, kind="ExternalInput").ap()
    od = nc.dram_tensor("o", [S, NHC * HD], BF16, kind="ExternalOutput").ap()

    iters = [(h, qb) for h in range(NHC) for qb in range(NQB)]
    NIT = len(iters)
    NKCs = [qb + 1 for (h, qb) in iters]
    NBs = [(n + 3) // 4 for n in NKCs]
    Sm = [0] * (NIT + 1)
    Sv = [0] * (NIT + 1)
    Sa = [0] * (NIT + 1)
    for it in range(NIT):
        Sm[it + 1] = Sm[it] + 1 + 2 * NKCs[it]
        Sv[it + 1] = Sv[it] + NKCs[it] + 6
        Sa[it + 1] = Sa[it] + NBs[it]
    scores_done = lambda it: Sm[it] + 1
    def ptt(it, kc):
        return Sm[it] + (kc + 2 if kc < 2 else 2 * kc + 1)
    def pv(it, kc):
        n = NKCs[it]
        if kc == n - 1:
            return Sm[it] + 2 * n + 1
        if kc == n - 2:
            return Sm[it] + 2 * n
        return Sm[it] + 4 + 2 * kc
    pv_done = lambda it: Sm[it] + 1 + 2 * NKCs[it]
    v_mask = lambda it: Sv[it] + 1
    v_m4 = lambda it: Sv[it] + 2
    v_negmax = lambda it: Sv[it] + 3
    v_ptev = lambda it, kc: Sv[it] + 3 + (kc + 1)
    v_den = lambda it: Sv[it] + 3 + NKCs[it] + 1
    v_recip = lambda it: Sv[it] + 3 + NKCs[it] + 2
    v_oev = lambda it: Sv[it] + 3 + NKCs[it] + 3
    exp_done = lambda it: Sa[it] + NBs[it]

    with ExitStack() as es:
        sb = lambda n, s, t: es.enter_context(nc.sbuf_tensor(n, s, t))
        ps = lambda n, s, t: es.enter_context(nc.psum_tensor(n, s, t))
        qs = sb("qs", [128, NHC * S], BF16)
        ks = sb("ks", [128, NKC * S], BF16)
        vs = sb("vs", [128, (S // 128) * NKC * HD], BF16)
        idt = sb("idt", [128, 128], BF16)
        mdt = sb("mdt", [128, 128], F32)
        P = [sb(f"P{i}", [128, S], BF16) for i in range(2)]
        pts = [sb(f"pts{i}", [128, 128], BF16) for i in range(2)]
        osb = [sb(f"osb{i}", [128, 128], BF16) for i in range(2)]
        m4 = sb("m4", [128, 4], F32)
        nm = sb("nm", [128, 1], F32)
        ac4 = sb("ac4", [128, 4], F32)
        den = sb("den", [128, 1], F32)
        rcp = [sb(f"rcp{i}", [128, 1], F32) for i in range(2)]
        sc = [ps(f"sc{i}", [128, 512], F32) for i in range(4)]
        ptp = [ps(f"ptp{i}", [128, 128], BF16) for i in range(2)]
        op = [ps(f"op{i}", [128, 128], F32) for i in range(2)]
        ld = es.enter_context(nc.semaphore("ld"))
        smm = es.enter_context(nc.semaphore("smm"))
        svv = es.enter_context(nc.semaphore("svv"))
        saa = es.enter_context(nc.semaphore("saa"))
        sst = [es.enter_context(nc.semaphore(f"sst{i}")) for i in range(2)]
        block = es.enter_context(nc.Block())

        @block.sync
        def _(sync):
            qr = qd.rearrange("(hc p) t -> p hc t", p=128)
            qo = qs.ap().rearrange("p (hc t) -> p hc t", hc=NHC)
            sync.dma_start(out=qo[:, 0:8, :], in_=qr[:, 0:8, :]).then_inc(ld, 16)
            sync.dma_start(out=qo[:, 8:16, :], in_=qr[:, 8:16, :]).then_inc(ld, 16)
            sync.dma_start(out=ks.ap().rearrange("p (hc t) -> p hc t", hc=NKC),
                           in_=kd.rearrange("(hc p) t -> p hc t", p=128)).then_inc(ld, 16)
            sync.dma_start(out=vs.ap().rearrange("p (tc d) -> p tc d", tc=S // 128),
                           in_=vd.rearrange("(tc p) d -> p tc d", p=128)).then_inc(ld, 16)
            sync.dma_start(out=idt[:, :], in_=idd[:, :]).then_inc(ld, 16)
            sync.dma_start(out=mdt[:, :], in_=mdd[:, :]).then_inc(ld, 16)
            sync.wait_ge(sst[0], 16 * ((NIT + 1) // 2))
            sync.wait_ge(sst[1], 16 * (NIT // 2))

        @block.tensor
        def _(tensor):
            tensor.wait_ge(ld, 96)
            for it, (h, qb) in enumerate(iters):
                bi = it % 2
                kvh = h // 4
                nkc = NKCs[it]
                nb = NBs[it]
                qt = qs[:, h * S + qb * 128: h * S + (qb + 1) * 128]
                if it >= 1:
                    tensor.wait_ge(saa, exp_done(it - 1))
                for f in range(nb):
                    lo = f * 512
                    ln_ = min(512, nkc * 128 - lo)
                    inst = tensor.matmul(sc[f][:, 0:ln_], qt,
                                         ks[:, kvh * S + lo: kvh * S + lo + ln_],
                                         start=True, stop=True)
                inst.then_inc(smm, 1)
                tensor.wait_ge(saa, exp_done(it))

                def emit_T(kc):
                    pb = kc % 2
                    if kc >= 2:
                        tensor.wait_ge(svv, v_ptev(it, kc - 2))
                    elif it >= 1:
                        tensor.wait_ge(svv, v_ptev(it - 1, NKCs[it - 1] - 1))
                    tensor.transpose(ptp[pb][:, :], P[bi][:, kc * 128:(kc + 1) * 128],
                                     idt[:, :]).then_inc(smm, 1)

                def emit_PV(kc):
                    if kc == 0 and it >= 2:
                        tensor.wait_ge(svv, v_oev(it - 2))
                    tensor.wait_ge(svv, v_ptev(it, kc))
                    tensor.matmul(op[bi][:, :], pts[kc % 2][:, :],
                                  vs[:, kc * NKC * HD + kvh * HD:
                                      kc * NKC * HD + (kvh + 1) * HD],
                                  start=(kc == 0), stop=(kc == nkc - 1),
                                  skip_group_check=True).then_inc(smm, 1)

                emit_T(0)
                if nkc > 1:
                    emit_T(1)
                for kc in range(nkc):
                    emit_PV(kc)
                    if kc + 2 < nkc:
                        emit_T(kc + 2)

        @block.vector
        def _(vector):
            vector.wait_ge(ld, 96)
            for it, (h, qb) in enumerate(iters):
                bi = it % 2
                nkc = NKCs[it]
                nb = NBs[it]
                dc = nkc - 1
                vector.wait_ge(smm, scores_done(it))
                dslice = sc[dc // 4][:, (dc % 4) * 128:(dc % 4) * 128 + 128]
                vector.tensor_tensor(dslice, dslice, mdt[:, :], ALU.add).then_inc(svv, 1)
                vector.wait_ge(svv, v_mask(it))
                for f in range(nb):
                    lo = f * 512
                    ln_ = min(512, nkc * 128 - lo)
                    inst = vector.reduce_max(m4[:, f:f + 1], sc[f][:, 0:ln_], axis=AX.X)
                inst.then_inc(svv, 1)
                vector.wait_ge(svv, v_m4(it))
                vector.reduce_max(nm[:, :], m4[:, 0:nb], axis=AX.X, negate=True).then_inc(svv, 1)
                for kc in range(nkc):
                    vector.wait_ge(smm, ptt(it, kc))
                    if kc >= 2:
                        vector.wait_ge(smm, pv(it, kc - 2))
                    elif it >= 1:
                        vector.wait_ge(smm, pv_done(it - 1))
                    vector.tensor_copy(pts[kc % 2][:, :], ptp[kc % 2][:, :]).then_inc(svv, 1)
                vector.wait_ge(saa, exp_done(it))
                vector.reduce_sum(den[:, :], ac4[:, 0:nb], axis=AX.X).then_inc(svv, 1)
                vector.wait_ge(svv, v_den(it))
                vector.reciprocal(rcp[bi][:, :], den[:, :]).then_inc(svv, 1)
                vector.wait_ge(smm, pv_done(it))
                vector.wait_ge(svv, v_recip(it))
                if it >= 2:
                    vector.wait_ge(sst[it % 2], 16 * (it // 2))
                vector.tensor_scalar(osb[bi][:, :], op[bi][:, :], rcp[bi][:, 0:1], None,
                                     ALU.mult).then_inc(svv, 1)

        @block.scalar
        def _(scalar):
            for it, (h, qb) in enumerate(iters):
                bi = it % 2
                nkc = NKCs[it]
                nb = NBs[it]
                scalar.wait_ge(svv, v_negmax(it))
                if it >= 2:
                    scalar.wait_ge(smm, ptt(it - 2, NKCs[it - 2] - 1))
                for f in range(nb):
                    lo = f * 512
                    ln_ = min(512, nkc * 128 - lo)
                    scalar.activation(P[bi][:, lo:lo + ln_], sc[f][:, 0:ln_], ACT.Exp,
                                      bias=nm[:, 0:1], accum_out=ac4[:, f:f + 1]).then_inc(saa, 1)
                scalar.wait_ge(svv, v_oev(it))
                scalar.dma_start(out=od[qb * 128:(qb + 1) * 128, h * HD:(h + 1) * HD],
                                 in_=osb[bi][:, :]).then_inc(sst[it % 2], 16)
    return nc


def launch_attn0(q_rope, k_rope, v_b, consts):
    nc = get_nc("attn0", build_attn0)
    ins = []
    for c in range(NCORES):
        b, hh = c // 2, c % 2
        qsl = np.ascontiguousarray(
            q_rope[b * S:(b + 1) * S, hh * 16 * HD:(hh + 1) * 16 * HD].T)
        ksl = np.ascontiguousarray(
            k_rope[b * S:(b + 1) * S, hh * 4 * HD:(hh + 1) * 4 * HD].T)
        vsl = np.ascontiguousarray(
            v_b[b * S:(b + 1) * S, hh * 4 * HD:(hh + 1) * 4 * HD])
        ins.append({"qf": qsl, "kf": ksl, "vt": vsl,
                    "ident": consts["ident"], "madd": consts["madd_causal"]})
    outs = run(nc, ins)
    o = np.zeros((MTOK, NH * HD), dtype=_BF)
    for c in range(NCORES):
        b, hh = c // 2, c % 2
        o[b * S:(b + 1) * S, hh * 16 * HD:(hh + 1) * 16 * HD] = outs[c]["o"]
    return o


def const_tiles():
    c = {}
    c["ident"] = np.eye(128, dtype=_BF)
    m = np.where(np.tril(np.ones((128, 128), bool)), 0.0, -1e30).astype(np.float32)
    c["madd_causal"] = m
    c["mask01"] = np.where(np.arange(128)[:, None] <= np.arange(128)[None, :], 1.0, 0.0).astype(np.float32)
    a = np.arange(128)[:, None]
    kk = np.arange(192)[None, :] - 64
    band = (kk <= a) & (kk >= a - 64)
    c["madd_swa"] = np.where(band, 0.0, -1e30).astype(np.float32)
    mc0 = band & (np.arange(192)[None, :] >= 64)
    c["madd_swa0"] = np.where(mc0, 0.0, -1e30).astype(np.float32)
    e = np.zeros((8, 128), dtype=_BF)
    for h in range(8):
        e[h, h * 16:(h + 1) * 16] = 16.0
    c["e16"] = e
    rm = np.zeros((128, 128), dtype=np.float32)
    for j in range(64):
        rm[j, j + 64] = -1.0
        rm[j + 64, j] = 1.0
    c["rotmT"] = rm.T.astype(_BF)
    return c



def build_add(y_dt=F32):
    nc = bass.Bass("TRN2", target_bir_lowering=False, debug=False)
    x = nc.dram_tensor("x", [MC, D], F32, kind="ExternalInput").ap()
    y = nc.dram_tensor("y", [MC, D], y_dt, kind="ExternalInput").ap()
    z = nc.dram_tensor("z", [MC, D], F32, kind="ExternalOutput").ap()
    nt = MC // 128
    with ExitStack() as es:
        sb = lambda n, s, t: es.enter_context(nc.sbuf_tensor(n, s, t))
        xt = [sb(f"xt{i}", [128, D], F32) for i in range(2)]
        yt = [sb(f"yt{i}", [128, D], y_dt) for i in range(2)]
        yf = [sb(f"yf{i}", [128, D], F32) for i in range(2)]
        zt = [sb(f"zt{i}", [128, D], F32) for i in range(2)]
        ld = [es.enter_context(nc.semaphore(f"ld{i}")) for i in range(2)]
        sv = es.enter_context(nc.semaphore("sv"))
        st = [es.enter_context(nc.semaphore(f"st{i}")) for i in range(2)]
        block = es.enter_context(nc.Block())
        C = Ctr()

        @block.sync
        def _(sync):
            for i in range(nt):
                b = i % 2
                if i >= 2:
                    sync.wait_ge(sv, 2 * (i - 1))
                sync.dma_start(out=xt[b][:, :], in_=x[128 * i:128 * (i + 1), :]).then_inc(ld[b], 16)
                sync.dma_start(out=yt[b][:, :], in_=y[128 * i:128 * (i + 1), :]).then_inc(ld[b], 16)
            sync.wait_ge(st[0], 16 * ((nt + 1) // 2))
            sync.wait_ge(st[1], 16 * (nt // 2))

        @block.vector
        def _(vector):
            for i in range(nt):
                b = i % 2
                vector.wait_ge(ld[b], 32 * (i // 2 + 1))
                vector.tensor_copy(yf[b][:, :], yt[b][:, :]).then_inc(sv, 1); C.inc(sv)
                vector.wait_ge(sv, C.val(sv))
                if i >= 2:
                    vector.wait_ge(st[b], 16 * (i // 2))
                vector.tensor_tensor(zt[b][:, :], xt[b][:, :], yf[b][:, :], ALU.add).then_inc(sv, 1)
                C.inc(sv)

        @block.scalar
        def _(scalar):
            for i in range(nt):
                b = i % 2
                scalar.wait_ge(sv, 2 * (i + 1))
                scalar.dma_start(out=z[128 * i:128 * (i + 1), :], in_=zt[b][:, :]).then_inc(st[b], 16)
    return nc


def launch_add(x, y):
    y_dt = {np.dtype(np.float32): F32, np.dtype(np.float16): F16,
            np.dtype(_BF): BF16}[np.dtype(y.dtype)]
    nc = get_nc(("add", str(y_dt)), build_add, y_dt)
    ins = [{"x": np.ascontiguousarray(x[MC * c:MC * (c + 1)]),
            "y": np.ascontiguousarray(y[MC * c:MC * (c + 1)])} for c in range(NCORES)]
    outs = run(nc, ins)
    return np.concatenate([o["z"] for o in outs], axis=0)



def build_dequant(NCH, KH, G):
    Kd = 2 * KH
    assert Kd == G * GS
    nc = bass.Bass("TRN2", target_bir_lowering=False, debug=False)
    btd = nc.dram_tensor("bt", [NCH * 128, KH], U8, kind="ExternalInput").ap()
    std_ = nc.dram_tensor("st", [NCH * 128, G], F16, kind="ExternalInput").ap()
    zbd = nc.dram_tensor("zb", [G, NCH * 64], U8, kind="ExternalInput").ap()
    idd = nc.dram_tensor("ident", [128, 128], BF16, kind="ExternalInput").ap()
    wd = nc.dram_tensor("w", [Kd, NCH * 128], BF16, kind="ExternalOutput").ap()

    VB = lambda cc: 4 + cc * (8 + 2 * G)
    v_stf = lambda cc: VB(cc) + 3
    v_zcomb = lambda cc: VB(cc) + 5
    v_zts = lambda cc: VB(cc) + 6
    v_dq = lambda cc, g: VB(cc) + 9 + 2 * g
    v_wkn = lambda cc, g: VB(cc) + 10 + 2 * g
    m_ztr = lambda cc: cc * (1 + G) + 1
    m_wtr = lambda cc, g: cc * (1 + G) + 1 + (g + 1)
    idx_of = lambda cc, g: cc * G + g
    def cg_of(idx):
        return idx // G, idx % G

    with ExitStack() as es:
        sb = lambda n, s, t: es.enter_context(nc.sbuf_tensor(n, s, t))
        ps = lambda n, s, t: es.enter_context(nc.psum_tensor(n, s, t))
        idt = sb("idt", [128, 128], BF16)
        zbs = sb("zbs", [G, NCH * 64], U8)
        zlo = sb("zlo", [G, NCH * 64], U8)
        zhi = sb("zhi", [G, NCH * 64], U8)
        zlob = sb("zlob", [G, NCH * 64], BF16)
        zhib = sb("zhib", [G, NCH * 64], BF16)
        bts = [sb(f"bts{i}", [128, KH], U8) for i in range(2)]
        lop = [sb(f"lop{i}", [128, KH], U8) for i in range(2)]
        hip = [sb(f"hip{i}", [128, KH], U8) for i in range(2)]
        sts = [sb(f"sts{i}", [128, G], F16) for i in range(2)]
        stf = [sb(f"stf{i}", [128, G], F32) for i in range(2)]
        zts = [sb(f"zts{i}", [128, G], F32) for i in range(2)]
        zps = [sb(f"zps{i}", [128, G], F32) for i in range(2)]
        zcomb = sb("zcomb", [G, 128], BF16)
        wnk = [sb(f"wnk{i}", [128, 128], BF16) for i in range(2)]
        wkn = [sb(f"wkn{i}", [128, 128], BF16) for i in range(2)]
        ztp = ps("ztp", [128, G], BF16)
        wtp = [ps(f"wtp{i}", [128, 128], BF16) for i in range(2)]
        ld = [es.enter_context(nc.semaphore(f"ld{i}")) for i in range(2)]
        lz = es.enter_context(nc.semaphore("lz"))
        sv = es.enter_context(nc.semaphore("sv"))
        sm = es.enter_context(nc.semaphore("sm"))
        st_ = [es.enter_context(nc.semaphore(f"st{i}")) for i in range(2)]
        block = es.enter_context(nc.Block())
        NIDX = NCH * G

        @block.sync
        def _(sync):
            sync.dma_start(out=idt[:, :], in_=idd[:, :]).then_inc(lz, 16)
            sync.dma_start(out=zbs[:, :], in_=zbd[:, :]).then_inc(lz, 16)
            for cc in range(NCH):
                b = cc % 2
                if cc >= 2:
                    sync.wait_ge(sv, v_dq(cc - 2, G - 1))
                sync.dma_start(out=bts[b][:, :],
                               in_=btd[128 * cc:128 * (cc + 1), :]).then_inc(ld[b], 16)
                sync.dma_start(out=sts[b][:, :],
                               in_=std_[128 * cc:128 * (cc + 1), :]).then_inc(ld[b], 16)
            sync.wait_ge(st_[0], 16 * ((NIDX + 1) // 2))
            sync.wait_ge(st_[1], 16 * (NIDX // 2))

        @block.vector
        def _(vector):
            vector.wait_ge(lz, 32)
            vector.tensor_scalar(zlo.ap().bitcast(U16), zbs.ap().bitcast(U16),
                                 0x0F0F, None, ALU.bitwise_and).then_inc(sv, 1)
            vector.tensor_scalar(zhi.ap().bitcast(U16), zbs.ap().bitcast(U16),
                                 4, 0x0F0F, ALU.logical_shift_right, ALU.bitwise_and).then_inc(sv, 1)
            vector.wait_ge(sv, 2)
            vector.tensor_copy(zlob[:, :], zlo[:, :]).then_inc(sv, 1)
            vector.tensor_copy(zhib[:, :], zhi[:, :]).then_inc(sv, 1)
            for cc in range(NCH):
                b = cc % 2
                vector.wait_ge(ld[b], 32 * (cc // 2 + 1))
                vector.tensor_scalar(lop[b].ap().bitcast(U16), bts[b].ap().bitcast(U16),
                                     0x0F0F, None, ALU.bitwise_and).then_inc(sv, 1)
                vector.tensor_scalar(hip[b].ap().bitcast(U16), bts[b].ap().bitcast(U16),
                                     4, 0x0F0F, ALU.logical_shift_right, ALU.bitwise_and).then_inc(sv, 1)
                vector.tensor_copy(stf[b][:, :], sts[b][:, :]).then_inc(sv, 1)
                if cc >= 1:
                    vector.wait_ge(sm, m_ztr(cc - 1))
                vector.wait_ge(sv, 4)
                vector.tensor_copy(zcomb[:, 0:64], zlob[:, 64 * cc:64 * cc + 64]).then_inc(sv, 1)
                vector.tensor_copy(zcomb[:, 64:128], zhib[:, 64 * cc:64 * cc + 64]).then_inc(sv, 1)
                vector.wait_ge(sm, m_ztr(cc))
                vector.tensor_copy(zts[b][:, :], ztp[0:128, 0:G]).then_inc(sv, 1)
                vector.wait_ge(sv, v_zts(cc))
                vector.tensor_scalar(zps[b][:, :], zts[b][:, :], 1.0, None, ALU.add).then_inc(sv, 1)
                vector.wait_ge(sv, v_zts(cc) + 1)
                vector.tensor_tensor(zps[b][:, :], zps[b][:, :], stf[b][:, :], ALU.mult).then_inc(sv, 1)
                vector.wait_ge(sv, v_zts(cc) + 2)
                for g in range(G):
                    idx = idx_of(cc, g)
                    wb = idx % 2
                    if idx >= 2:
                        cc2, g2 = cg_of(idx - 2)
                        vector.wait_ge(sm, m_wtr(cc2, g2))
                    vector.tensor_scalar(wnk[wb][:, 0:64], lop[b][:, 64 * g:64 * g + 64],
                                         stf[b][:, g:g + 1], zps[b][:, g:g + 1],
                                         ALU.mult, ALU.subtract)
                    vector.tensor_scalar(wnk[wb][:, 64:128], hip[b][:, 64 * g:64 * g + 64],
                                         stf[b][:, g:g + 1], zps[b][:, g:g + 1],
                                         ALU.mult, ALU.subtract).then_inc(sv, 1)
                    vector.wait_ge(sm, m_wtr(cc, g))
                    if idx >= 2:
                        vector.wait_ge(st_[idx % 2], 16 * (idx // 2))
                    vector.tensor_copy(wkn[wb][:, :], wtp[wb][:, :]).then_inc(sv, 1)

        @block.tensor
        def _(tensor):
            tensor.wait_ge(lz, 32)
            tensor.wait_ge(sv, 4)
            for cc in range(NCH):
                b = cc % 2
                if cc >= 1:
                    tensor.wait_ge(sv, v_zts(cc - 1))
                tensor.wait_ge(sv, v_zcomb(cc))
                tensor.transpose(ztp[0:128, 0:G], zcomb[:, :],
                                 idt[0:G, 0:G]).then_inc(sm, 1)
                for g in range(G):
                    idx = idx_of(cc, g)
                    wb = idx % 2
                    tensor.wait_ge(sv, v_dq(cc, g))
                    if idx >= 2:
                        cc2, g2 = cg_of(idx - 2)
                        tensor.wait_ge(sv, v_wkn(cc2, g2))
                    tensor.transpose(wtp[wb][:, :], wnk[wb][:, :], idt[:, :]).then_inc(sm, 1)

        @block.scalar
        def _(scalar):
            for cc in range(NCH):
                for g in range(G):
                    idx = idx_of(cc, g)
                    wb = idx % 2
                    scalar.wait_ge(sv, v_wkn(cc, g))
                    scalar.dma_start(out=wd[128 * g:128 * (g + 1), 128 * cc:128 * (cc + 1)],
                                     in_=wkn[wb][:, :]).then_inc(st_[idx % 2], 16)
    return nc


def launch_dequant(qw, scales, zeros, n_pad):
    Kd = qw.shape[0] * 8
    N = qw.shape[1]
    G = scales.shape[0]
    KH = Kd // 2
    Npad = N + n_pad
    assert Npad % (NCORES * 128) == 0
    NCH = Npad // (NCORES * 128)
    bt = qweight_bytes(qw).T
    if n_pad:
        bt = np.concatenate([bt, np.zeros((n_pad, KH), np.uint8)], 0)
    pp = perm_plane(Npad)
    bt = np.ascontiguousarray(bt[pp])
    stt = scales.T.astype(np.float16, copy=False)
    if n_pad:
        stt = np.concatenate([stt, np.zeros((n_pad, G), np.float16)], 0)
    stt = np.ascontiguousarray(stt[pp])
    zb = zeros_bytes(zeros)
    if n_pad:
        zb = np.concatenate([zb, np.zeros((G, n_pad // 2), np.uint8)], 1)
    ident = np.eye(128, dtype=_BF)
    nc = get_nc(("dequant", NCH, KH, G), build_dequant, NCH, KH, G)
    ins = []
    ncols = NCH * 128
    for c in range(NCORES):
        ins.append({"bt": np.ascontiguousarray(bt[ncols * c:ncols * (c + 1)]),
                    "st": np.ascontiguousarray(stt[ncols * c:ncols * (c + 1)]),
                    "zb": np.ascontiguousarray(zb[:, ncols // 2 * c:ncols // 2 * (c + 1)]),
                    "ident": ident})
    outs = run(nc, ins)
    w = np.concatenate([o["w"] for o in outs], axis=1)
    inv = np.argsort(pp)
    w = w[:, inv]
    return np.ascontiguousarray(w[:, :N])



def build_gemm_bigk(K, N, out_dt=F16):
    nk = K // 128
    nn = N // 128
    ngrp = 4
    ng = nn // ngrp
    nc = bass.Bass("TRN2", target_bir_lowering=False, debug=False)
    xtd = nc.dram_tensor("xt", [K, MC], BF16, kind="ExternalInput").ap()
    wd = nc.dram_tensor("w", [K, N], BF16, kind="ExternalInput").ap()
    yd = nc.dram_tensor("y", [N, MC], out_dt, kind="ExternalOutput").ap()
    with ExitStack() as es:
        sb = lambda n, s, t: es.enter_context(nc.sbuf_tensor(n, s, t))
        ps = lambda n, s, t: es.enter_context(nc.psum_tensor(n, s, t))
        xts = [sb(f"xts{i}", [128, MC], BF16) for i in range(2)]
        wts = [sb(f"wts{i}", [128, ngrp * 128], BF16) for i in range(2)]
        yo = [sb(f"yo{i}", [128, MC], out_dt) for i in range(2)]
        pp = [ps(f"pp{i}", [128, 512], F32) for i in range(8)]
        lx = [es.enter_context(nc.semaphore(f"lx{i}")) for i in range(2)]
        lw = [es.enter_context(nc.semaphore(f"lw{i}")) for i in range(2)]
        mm = es.enter_context(nc.semaphore("mm"))
        ev = es.enter_context(nc.semaphore("ev"))
        st = [es.enter_context(nc.semaphore(f"st{i}")) for i in range(2)]
        block = es.enter_context(nc.Block())
        mm_done = lambda g, kk: g * nk + kk + 1

        @block.sync
        def _(sync):
            for g in range(ng):
                for kk in range(nk):
                    t = g * nk + kk
                    b = t % 2
                    if t >= 2:
                        sync.wait_ge(mm, 8 * (t - 1))
                    sync.dma_start(out=xts[b][:, :],
                                   in_=xtd[128 * kk:128 * (kk + 1), :]).then_inc(lx[b], 16)
                    sync.dma_start(out=wts[b][:, :],
                                   in_=wd[128 * kk:128 * (kk + 1),
                                          512 * g:512 * (g + 1)]).then_inc(lw[b], 16)
            sync.wait_ge(st[0], 16 * ngrp * ((ng + 1) // 2))
            sync.wait_ge(st[1], 16 * ngrp * (ng // 2))

        @block.tensor
        def _(tensor):
            for g in range(ng):
                for kk in range(nk):
                    t = g * nk + kk
                    b = t % 2
                    tensor.wait_ge(lx[b], 16 * (t // 2 + 1))
                    tensor.wait_ge(lw[b], 16 * (t // 2 + 1))
                    if kk == 0 and g >= 1:
                        tensor.wait_ge(ev, 8 * g)
                    for j in range(ngrp):
                        for m in range(2):
                            tensor.matmul(pp[2 * j + m][:, :],
                                          wts[b][:, 128 * j:128 * (j + 1)],
                                          xts[b][:, 512 * m:512 * (m + 1)],
                                          start=(kk == 0), stop=(kk == nk - 1),
                                          skip_group_check=True).then_inc(mm, 1)

        @block.scalar
        def _(scalar):
            for g in range(ng):
                b = g % 2
                scalar.wait_ge(mm, 8 * nk * (g + 1))
                if g >= 2:
                    scalar.wait_ge(st[b], 16 * (g // 2))
                for j in range(ngrp):
                    if (g // 2) * ngrp + j >= 1:
                        scalar.wait_ge(st[b], 16 * ((g // 2) * ngrp + j))
                    for m in range(2):
                        scalar.copy(yo[b][:, 512 * m:512 * (m + 1)], pp[2 * j + m][:, :]).then_inc(ev, 1)
                    scalar.wait_ge(ev, 8 * g + 2 * (j + 1))
                    scalar.dma_start(out=yd[512 * g + 128 * j:512 * g + 128 * (j + 1), :],
                                     in_=yo[b][:, :]).then_inc(st[b], 16)
    return nc


def launch_gemm_bigk(xt, w, out_dt=F16):
    K, N = w.shape
    nc = get_nc(("gemmbigk", K, N, str(out_dt)), build_gemm_bigk, K, N, out_dt)
    w = np.ascontiguousarray(w)
    ins = [{"xt": np.ascontiguousarray(xt[:, MC * c:MC * (c + 1)]), "w": w}
           for c in range(NCORES)]
    outs = run(nc, ins)
    return np.concatenate([o["y"] for o in outs], axis=1)



def build_silumul():
    nr = INTER // 128
    nc = bass.Bass("TRN2", target_bir_lowering=False, debug=False)
    gd = nc.dram_tensor("g", [INTER, MC], BF16, kind="ExternalInput").ap()
    ud = nc.dram_tensor("u", [INTER, MC], BF16, kind="ExternalInput").ap()
    zd = nc.dram_tensor("z", [INTER, MC], BF16, kind="ExternalOutput").ap()
    with ExitStack() as es:
        sb = lambda n, s, t: es.enter_context(nc.sbuf_tensor(n, s, t))
        gt = [sb(f"gt{i}", [128, MC], BF16) for i in range(2)]
        ut = [sb(f"ut{i}", [128, MC], BF16) for i in range(2)]
        gs = [sb(f"gs{i}", [128, MC], F32) for i in range(2)]
        zt = [sb(f"zt{i}", [128, MC], BF16) for i in range(2)]
        ld = [es.enter_context(nc.semaphore(f"ld{i}")) for i in range(2)]
        sa = es.enter_context(nc.semaphore("sa"))
        sv = es.enter_context(nc.semaphore("sv"))
        st = [es.enter_context(nc.semaphore(f"st{i}")) for i in range(2)]
        block = es.enter_context(nc.Block())

        @block.sync
        def _(sync):
            for i in range(nr):
                b = i % 2
                if i >= 2:
                    sync.wait_ge(sv, i - 1)
                sync.dma_start(out=gt[b][:, :], in_=gd[128 * i:128 * (i + 1), :]).then_inc(ld[b], 16)
                sync.dma_start(out=ut[b][:, :], in_=ud[128 * i:128 * (i + 1), :]).then_inc(ld[b], 16)
            sync.wait_ge(st[0], 16 * ((nr + 1) // 2))
            sync.wait_ge(st[1], 16 * (nr // 2))

        @block.scalar
        def _(scalar):
            for i in range(nr):
                b = i % 2
                scalar.wait_ge(ld[b], 32 * (i // 2 + 1))
                if i >= 2:
                    scalar.wait_ge(sv, i - 1)
                scalar.activation(gs[b][:, :], gt[b][:, :], ACT.Silu).then_inc(sa, 1)
                scalar.wait_ge(sv, i + 1)
                scalar.dma_start(out=zd[128 * i:128 * (i + 1), :], in_=zt[b][:, :]).then_inc(st[b], 16)

        @block.vector
        def _(vector):
            for i in range(nr):
                b = i % 2
                vector.wait_ge(sa, i + 1)
                if i >= 2:
                    vector.wait_ge(st[b], 16 * (i // 2))
                vector.tensor_tensor(zt[b][:, :], gs[b][:, :], ut[b][:, :], ALU.mult).then_inc(sv, 1)
    return nc


def launch_silumul(g, u):
    nc = get_nc("silumul", build_silumul)
    ins = [{"g": np.ascontiguousarray(g[:, MC * c:MC * (c + 1)]),
            "u": np.ascontiguousarray(u[:, MC * c:MC * (c + 1)])} for c in range(NCORES)]
    outs = run(nc, ins)
    return np.concatenate([o["z"] for o in outs], axis=1)




class AutoSched:
    def __init__(self):
        self.steps = []
        self.M = {}
        self.bufw = {}
        self.bufr = {}
        self.eng_of = {}

    def add(self, eng, fn, reads=(), writes=(), sem=None, amount=1, extra_wait_ids=()):
        sid = len(self.steps)
        waits = set(extra_wait_ids)
        for b in reads:
            w = self.bufw.get(b)
            if w is not None:
                waits.add(w)
            self.bufr.setdefault(b, []).append(sid)
        for b in writes:
            for r in self.bufr.get(b, ()):
                waits.add(r)
            w = self.bufw.get(b)
            if w is not None:
                waits.add(w)
            self.bufw[b] = sid
            self.bufr[b] = []
        waits.discard(sid)
        if sem is None:
            sem = {"tensor": "SM", "vector": "SV", "scalar": "SA", "gpsimd": "SG"}[eng]
        self.steps.append((eng, fn, sorted(waits), sem, amount))
        self.eng_of[sid] = eng
        return sid

    def plan(self):
        ctr = {}
        for sid, (eng, fn, waits, sem, amount) in enumerate(self.steps):
            ctr[sem] = ctr.get(sem, 0) + amount
            self.M[sid] = (sem, ctr[sem])

    def emit(self, eng_name, eng, sems):
        for sid, (seng, fn, waits, sem, amount) in enumerate(self.steps):
            if seng != eng_name:
                continue
            per_sem = {}
            for w in waits:
                s, v = self.M[w]
                per_sem[s] = max(per_sem.get(s, 0), v)
            for s, v in per_sem.items():
                if s == sem and v >= self.M[sid][1]:
                    continue
                eng.wait_ge(sems[s], v)
            inst = fn(eng)
            if inst is not None:
                inst.then_inc(sems[sem], amount)



def build_attn1():
    NHC, NKC = 16, 4
    NCHK = S // 128
    nc = bass.Bass("TRN2", target_bir_lowering=False, debug=False)
    qd = nc.dram_tensor("qf", [NHC * HD, S], BF16, kind="ExternalInput").ap()
    kd = nc.dram_tensor("kf", [NKC * HD, S], BF16, kind="ExternalInput").ap()
    vd = nc.dram_tensor("vt", [S, NKC * HD], BF16, kind="ExternalInput").ap()
    v2d = nc.dram_tensor("vt2", [S, NKC * HD], BF16, kind="ExternalInput").ap()
    idd = nc.dram_tensor("ident", [128, 128], BF16, kind="ExternalInput").ap()
    rmd = nc.dram_tensor("rotmT", [128, 128], BF16, kind="ExternalInput").ap()
    m01d = nc.dram_tensor("mask01", [128, 128], F32, kind="ExternalInput").ap()
    mswd = nc.dram_tensor("madd_swa", [128, 192], F32, kind="ExternalInput").ap()
    msw0d = nc.dram_tensor("madd_swa0", [128, 192], F32, kind="ExternalInput").ap()
    cqd = nc.dram_tensor("cosq_fm", [128, S], F32, kind="ExternalInput").ap()
    sqd = nc.dram_tensor("sinq_fm", [128, S], F32, kind="ExternalInput").ap()
    ckd = nc.dram_tensor("cosk_fm", [128, S], F32, kind="ExternalInput").ap()
    skd = nc.dram_tensor("sink_fm", [128, S], F32, kind="ExternalInput").ap()
    od = nc.dram_tensor("o", [NHC * HD, S], BF16, kind="ExternalOutput").ap()
    NLOAD = 14

    with ExitStack() as es:
        sb = lambda n, s, t: es.enter_context(nc.sbuf_tensor(n, s, t))
        psa = lambda n, s, t: es.enter_context(nc.psum_tensor(n, s, t))
        qs_ = sb("qs", [128, NHC * S], BF16)
        ks_ = sb("ks", [128, NKC * S], BF16)
        vs_ = sb("vs", [128, NCHK * NKC * HD], BF16)
        vs2 = sb("vs2", [128, NCHK * NKC * HD], BF16)
        idt = sb("idt", [128, 128], BF16)
        rmt = sb("rmt", [128, 128], BF16)
        m01 = sb("m01", [128, 128], F32)
        msw = sb("msw", [128, 192], F32)
        msw0 = sb("msw0", [128, 192], F32)
        cq = sb("cq", [128, S], F32)
        sq_ = sb("sq", [128, S], F32)
        ck = sb("ck", [128, S], F32)
        sk_ = sb("sk", [128, S], F32)
        Sst = sb("Sst", [128, NKC * 128], F32)
        Sbf = sb("Sbf", [128, NKC * 128], BF16)
        expG = [sb(f"expG{i}", [128, 128], F32) for i in range(NKC)]
        KEf = [sb(f"KEf{i}", [128, 128], BF16) for i in range(NKC)]
        skr = [[sb(f"skr{i}_{j}", [128, 128], BF16) for j in range(2)] for i in range(NKC)]
        ktm = sb("ktm", [128, 128], BF16)
        ek = sb("ek", [128, 128], F32)
        kstm = sb("kstm", [128, 128], BF16)
        ksf = sb("ksf", [128, 128], BF16)
        sig = sb("sig", [128, 128], F32)
        lsg = sb("lsg", [128, 128], F32)
        cumP = sb("cumP", [128, 128], F32)
        e2 = sb("e2", [128, 128], F32)
        e3 = sb("e3", [128, 128], F32)
        ke2f = sb("ke2f", [128, 128], BF16)
        ke2t = sb("ke2t", [128, 128], BF16)
        dC = sb("dC", [128, 1], F32)
        nmk = sb("nmk", [128, 1], F32)
        denk = sb("denk", [128, 1], F32)
        rcpk = sb("rcpk", [128, 1], F32)
        rkp1 = sb("rkp1", [128, 128], F32)
        rkp2 = sb("rkp2", [128, 128], F32)
        qtm = sb("qtm", [128, 128], BF16)
        eqv = sb("eqv", [128, 128], F32)
        qsm = sb("qsm", [128, 128], BF16)
        qsf = sb("qsf", [128, 128], BF16)
        QEf = sb("QEf", [128, 128], BF16)
        nmq = sb("nmq", [128, 1], F32)
        denq = sb("denq", [128, 1], F32)
        rcpq = sb("rcpq", [128, 1], F32)
        Am = sb("Am", [128, 128], BF16)
        rqp1 = sb("rqp1", [128, 128], F32)
        rqp2 = sb("rqp2", [128, 128], F32)
        sqf = sb("sqf", [128, 128], BF16)
        Pn = sb("Pn", [128, 192], BF16)
        nms = sb("nms", [128, 1], F32)
        dens = sb("dens", [128, 1], F32)
        rcps = sb("rcps", [128, 1], F32)
        est = sb("est", [128, 192], F32)
        pta = sb("pta", [64, 128], BF16)
        ptb = sb("ptb", [128, 128], BF16)
        osb = [sb(f"osb{i}", [128, 128], BF16) for i in range(2)]
        tp = [psa(f"tp{i}", [128, 128], BF16) for i in range(2)]
        rpp = psa("rpp", [128, 128], F32)
        app = psa("app", [128, 128], F32)
        spp = psa("spp", [128, 192], F32)
        upp = psa("upp", [128, 128], F32)
        opp = [psa(f"opp{i}", [128, 128], F32) for i in range(2)]
        sems = {}
        for name in ("ld", "SM", "SV", "SA", "SS0", "SS1"):
            sems[name] = es.enter_context(nc.semaphore(name))
        block = es.enter_context(nc.Block())

        A = AutoSched()
        tpi = {"i": 0}
        A.add("vector", lambda v: v.memset(Sst[:, :], 0.0),
              writes=[("Sst", k) for k in range(NKC)])

        def TRE(in_fn, in_reads, out_ap_fn, out_buf, part=128):
            i = tpi["i"]; tpi["i"] += 1
            pb = i % 2
            A.add("tensor", lambda t, pb=pb, f=in_fn, part=part:
                  t.transpose(tp[pb][0:part, :], f(), idt[:, :]),
                  reads=list(in_reads), writes=[("tp", pb)])
            A.add("vector", lambda v, pb=pb, f=out_ap_fn, part=part:
                  v.tensor_copy(f(), tp[pb][0:part, :]),
                  reads=[("tp", pb)], writes=[out_buf])

        for c in range(NCHK):
            for kvh in range(NKC):
                kc_ap = lambda kvh=kvh, c=c: ks_[:, kvh * S + c * 128:kvh * S + (c + 1) * 128]
                kbuf = ("ks", kvh, c)
                TRE(kc_ap, [kbuf], lambda: ktm[:, :], "ktm")
                A.add("scalar", lambda a, kc_ap=kc_ap: a.activation(sig[:, :], kc_ap(), ACT.Sigmoid),
                      reads=[kbuf], writes=["sig"])
                A.add("scalar", lambda a: a.activation(lsg[:, :], sig[:, :], ACT.Ln),
                      reads=["sig"], writes=["lsg"])
                A.add("vector", lambda v: v.tensor_tensor_scan(cumP[:, :], lsg[:, :], lsg[:, :], 0.0, ALU.add, ALU.bypass),
                      reads=["lsg"], writes=["cumP"])
                A.add("scalar", lambda a, kvh=kvh: a.activation(expG[kvh][:, :], cumP[:, :], ACT.Exp, scale=1.0 / 16.0),
                      reads=["cumP"], writes=[("expG", kvh)])
                A.add("scalar", lambda a: a.activation(e2[:, :], cumP[:, :], ACT.Exp, scale=-1.0 / 16.0),
                      reads=["cumP"], writes=["e2"])
                A.add("scalar", lambda a: a.activation(dC[:, :], cumP[:, 127:128], ACT.Exp, scale=1.0 / 16.0),
                      reads=["cumP"], writes=["dC"])
                A.add("vector", lambda v: v.tensor_scalar(e3[:, :], cumP[:, :], cumP[:, 127:128], None, ALU.subtract),
                      reads=["cumP"], writes=["e3"])
                A.add("scalar", lambda a: a.activation(e3[:, :], e3[:, :], ACT.Exp, scale=-1.0 / 16.0),
                      reads=["e3"], writes=["e3"])
                A.add("vector", lambda v: v.reduce_max(nmk[:, :], ktm[:, :], axis=AX.X, negate=True),
                      reads=["ktm"], writes=["nmk"])
                A.add("scalar", lambda a: a.activation(ek[:, :], ktm[:, :], ACT.Exp, bias=nmk[:, 0:1], accum_out=denk[:, :]),
                      reads=["ktm", "nmk"], writes=["ek", "denk"])
                A.add("vector", lambda v: v.reciprocal(rcpk[:, :], denk[:, :]),
                      reads=["denk"], writes=["rcpk"])
                A.add("vector", lambda v: v.tensor_scalar(kstm[:, :], ek[:, :], rcpk[:, 0:1], None, ALU.mult),
                      reads=["ek", "rcpk"], writes=["kstm"])
                TRE(lambda: kstm[:, :], ["kstm"], lambda: ksf[:, :], "ksf")
                A.add("vector", lambda v, kvh=kvh: v.tensor_tensor(KEf[kvh][:, :], ksf[:, :], e2[:, :], ALU.mult),
                      reads=["ksf", "e2"], writes=[("KEf", kvh)])
                A.add("vector", lambda v: v.tensor_tensor(ke2f[:, :], ksf[:, :], e3[:, :], ALU.mult),
                      reads=["ksf", "e3"], writes=["ke2f"])
                TRE(lambda: ke2f[:, :], ["ke2f"], lambda: ke2t[:, :], "ke2t")
                A.add("tensor", lambda t, kc_ap=kc_ap: t.matmul(rpp[:, :], rmt[:, :], kc_ap(), start=True, stop=True),
                      reads=[kbuf], writes=["rpp"])
                A.add("vector", lambda v, c=c, kc_ap=kc_ap: v.tensor_tensor(rkp1[:, :], ck[:, c * 128:(c + 1) * 128], kc_ap(), ALU.mult),
                      reads=[kbuf], writes=["rkp1"])
                A.add("vector", lambda v, c=c: v.scalar_tensor_tensor(rkp2[:, :], rpp[:, :], 1.0, sk_[:, c * 128:(c + 1) * 128], ALU.mult, ALU.mult),
                      reads=["rpp"], writes=["rkp2"])
                A.add("vector", lambda v, kvh=kvh, c=c: v.tensor_tensor(skr[kvh][c % 2][:, :], rkp1[:, :], rkp2[:, :], ALU.add),
                      reads=["rkp1", "rkp2"], writes=[("skr", kvh, c % 2)])
                A.add("vector", lambda v, kvh=kvh: v.tensor_copy(Sbf[:, kvh * 128:(kvh + 1) * 128], Sst[:, kvh * 128:(kvh + 1) * 128]),
                      reads=[("Sst", kvh)], writes=[("Sbf", kvh)])
                A.add("tensor", lambda t, kvh=kvh, c=c: t.matmul(upp[:, :], ke2t[:, :], vs_[:, (c * NKC + kvh) * HD:(c * NKC + kvh + 1) * HD], start=True, stop=True),
                      reads=["ke2t"], writes=["upp"])
                A.add("vector", lambda v, kvh=kvh: v.scalar_tensor_tensor(Sst[:, kvh * 128:(kvh + 1) * 128], Sst[:, kvh * 128:(kvh + 1) * 128], dC[:, 0:1], upp[:, :], ALU.mult, ALU.add),
                      reads=[("Sst", kvh), "dC", "upp"], writes=[("Sst", kvh)])
            for h in range(NHC):
                kvh = h // 4
                it = c * NHC + h
                ob = it % 2
                qc_ap = lambda h=h, c=c: qs_[:, h * S + c * 128:h * S + (c + 1) * 128]
                qbuf = ("qs", h, c)
                TRE(qc_ap, [qbuf], lambda: qtm[:, :], "qtm")
                A.add("vector", lambda v: v.reduce_max(nmq[:, :], qtm[:, :], axis=AX.X, negate=True),
                      reads=["qtm"], writes=["nmq"])
                A.add("scalar", lambda a: a.activation(eqv[:, :], qtm[:, :], ACT.Exp, bias=nmq[:, 0:1], accum_out=denq[:, :]),
                      reads=["qtm", "nmq"], writes=["eqv", "denq"])
                A.add("vector", lambda v: v.reciprocal(rcpq[:, :], denq[:, :]),
                      reads=["denq"], writes=["rcpq"])
                A.add("vector", lambda v: v.tensor_scalar(rcpq[:, :], rcpq[:, :], 0.5, None, ALU.mult),
                      reads=["rcpq"], writes=["rcpq"])
                A.add("vector", lambda v: v.tensor_scalar(qsm[:, :], eqv[:, :], rcpq[:, 0:1], None, ALU.mult),
                      reads=["eqv", "rcpq"], writes=["qsm"])
                TRE(lambda: qsm[:, :], ["qsm"], lambda: qsf[:, :], "qsf")
                A.add("vector", lambda v, kvh=kvh: v.tensor_tensor(QEf[:, :], qsf[:, :], expG[kvh][:, :], ALU.mult),
                      reads=["qsf", ("expG", kvh)], writes=["QEf"])
                A.add("tensor", lambda t, kvh=kvh: t.matmul(app[:, :], KEf[kvh][:, :], QEf[:, :], start=True, stop=True),
                      reads=[("KEf", kvh), "QEf"], writes=["app"])
                A.add("vector", lambda v: v.tensor_tensor(Am[:, :], app[:, :], m01[:, :], ALU.mult),
                      reads=["app"], writes=["Am"])
                A.add("tensor", lambda t, qc_ap=qc_ap: t.matmul(rpp[:, :], rmt[:, :], qc_ap(), start=True, stop=True),
                      reads=[qbuf], writes=["rpp"])
                A.add("vector", lambda v, c=c, qc_ap=qc_ap: v.tensor_tensor(rqp1[:, :], cq[:, c * 128:(c + 1) * 128], qc_ap(), ALU.mult),
                      reads=[qbuf], writes=["rqp1"])
                A.add("vector", lambda v, c=c: v.scalar_tensor_tensor(rqp2[:, :], rpp[:, :], 1.0, sq_[:, c * 128:(c + 1) * 128], ALU.mult, ALU.mult),
                      reads=["rpp"], writes=["rqp2"])
                A.add("vector", lambda v: v.tensor_tensor(sqf[:, :], rqp1[:, :], rqp2[:, :], ALU.add),
                      reads=["rqp1", "rqp2"], writes=["sqf"])
                if c == 0:
                    A.add("vector", lambda v: v.memset(spp[:, 0:64], 0.0),
                          writes=[("spp", 0)])
                else:
                    A.add("tensor", lambda t, kvh=kvh, c=c: t.matmul(spp[:, 0:64], sqf[:, :], skr[kvh][(c - 1) % 2][:, 64:128], start=True, stop=True),
                          reads=["sqf", ("skr", kvh, (c - 1) % 2)], writes=[("spp", 0)])
                A.add("tensor", lambda t, kvh=kvh, c=c: t.matmul(spp[:, 64:192], sqf[:, :], skr[kvh][c % 2][:, :], start=True, stop=True),
                      reads=["sqf", ("skr", kvh, c % 2)], writes=[("spp", 1)])
                A.add("vector", lambda v, c=c: v.tensor_tensor(spp[:, :], spp[:, :], msw[:, :] if c > 0 else msw0[:, :], ALU.add),
                      reads=[("spp", 0), ("spp", 1)], writes=[("spp", 0), ("spp", 1)])
                A.add("vector", lambda v: v.reduce_max(nms[:, :], spp[:, :], axis=AX.X, negate=True),
                      reads=[("spp", 0), ("spp", 1)], writes=["nms"])
                A.add("scalar", lambda a: a.activation(est[:, :], spp[:, :], ACT.Exp, bias=nms[:, 0:1], accum_out=dens[:, :]),
                      reads=[("spp", 0), ("spp", 1), "nms"], writes=["est", "dens"])
                A.add("vector", lambda v: v.reciprocal(rcps[:, :], dens[:, :]),
                      reads=["dens"], writes=["rcps"])
                A.add("vector", lambda v: v.tensor_scalar(rcps[:, :], rcps[:, :], 0.5, None, ALU.mult),
                      reads=["rcps"], writes=["rcps"])
                A.add("vector", lambda v: v.tensor_scalar(Pn[:, :], est[:, :], rcps[:, 0:1], None, ALU.mult),
                      reads=["est", "rcps"], writes=["Pn"])
                if c > 0:
                    TRE(lambda: Pn[:, 0:64], ["Pn"], lambda: pta[:, :], "pta", part=64)
                TRE(lambda: Pn[:, 64:192], ["Pn"], lambda: ptb[:, :], "ptb")
                A.add("tensor", lambda t, kvh=kvh, ob=ob: t.matmul(opp[ob][:, :], Sbf[:, kvh * 128:(kvh + 1) * 128], QEf[:, :], start=True, stop=False, skip_group_check=True),
                      reads=[("Sbf", kvh), "QEf"], writes=[("opp", ob)])
                A.add("tensor", lambda t, ob=ob, kvh=kvh, c=c: t.matmul(opp[ob][:, :], vs_[:, (c * NKC + kvh) * HD:(c * NKC + kvh + 1) * HD], Am[:, :], start=False, stop=False, skip_group_check=True),
                      reads=["Am"], writes=[("opp", ob)])
                if c > 0:
                    A.add("tensor", lambda t, ob=ob, kvh=kvh, c=c: t.matmul(opp[ob][:, :], vs2[:, (c * NKC + kvh) * HD:(c * NKC + kvh) * HD + HD][0:64, :], pta[:, :], start=False, stop=False, skip_group_check=True),
                          reads=["pta"], writes=[("opp", ob)])
                A.add("tensor", lambda t, ob=ob, kvh=kvh, c=c: t.matmul(opp[ob][:, :], vs_[:, (c * NKC + kvh) * HD:(c * NKC + kvh + 1) * HD], ptb[:, :], start=False, stop=True, skip_group_check=True),
                      reads=["ptb"], writes=[("opp", ob)])
                A.add("vector", lambda v, ob=ob: v.tensor_copy(osb[ob][:, :], opp[ob][:, :]),
                      reads=[("opp", ob)], writes=[("osb", ob)])
                A.add("scalar", lambda a, h=h, c=c, ob=ob: a.dma_start(out=od[h * HD:(h + 1) * HD, c * 128:(c + 1) * 128], in_=osb[ob][:, :]),
                      reads=[("osb", ob)], sem=f"SS{it % 2}", amount=16)

        A.plan()
        NIT = NCHK * NHC

        @block.sync
        def _(sync):
            for (dst, src) in ((idt, idd), (rmt, rmd), (m01, m01d), (msw, mswd), (msw0, msw0d),
                               (cq, cqd), (sq_, sqd), (ck, ckd), (sk_, skd)):
                sync.dma_start(out=dst[:, :], in_=src[:, :]).then_inc(sems["ld"], 16)
            qr = qd.rearrange("(hc p) t -> p hc t", p=128)
            qo = qs_.ap().rearrange("p (hc t) -> p hc t", hc=NHC)
            sync.dma_start(out=qo[:, 0:8, :], in_=qr[:, 0:8, :]).then_inc(sems["ld"], 16)
            sync.dma_start(out=qo[:, 8:16, :], in_=qr[:, 8:16, :]).then_inc(sems["ld"], 16)
            sync.dma_start(out=ks_.ap().rearrange("p (hc t) -> p hc t", hc=NKC),
                           in_=kd.rearrange("(hc p) t -> p hc t", p=128)).then_inc(sems["ld"], 16)
            sync.dma_start(out=vs_.ap().rearrange("p (tc d) -> p tc d", tc=NCHK),
                           in_=vd.rearrange("(tc p) d -> p tc d", p=128)).then_inc(sems["ld"], 16)
            sync.dma_start(out=vs2.ap().rearrange("p (tc d) -> p tc d", tc=NCHK),
                           in_=v2d.rearrange("(tc p) d -> p tc d", p=128)).then_inc(sems["ld"], 16)
            sync.wait_ge(sems["SS0"], 16 * ((NIT + 1) // 2))
            sync.wait_ge(sems["SS1"], 16 * (NIT // 2))

        @block.vector
        def _(vector):
            vector.wait_ge(sems["ld"], 16 * NLOAD)
            A.emit("vector", vector, sems)

        @block.scalar
        def _(scalar):
            scalar.wait_ge(sems["ld"], 16 * NLOAD)
            A.emit("scalar", scalar, sems)

        @block.tensor
        def _(tensor):
            tensor.wait_ge(sems["ld"], 16 * NLOAD)
            A.emit("tensor", tensor, sems)
    return nc



def build_ffn1():
    nk = D // 128
    nc = bass.Bass("TRN2", target_bir_lowering=False, debug=False)
    xtd = nc.dram_tensor("xt", [D, MC], BF16, kind="ExternalInput").ap()
    acd = nc.dram_tensor("acat", [D, 128], F32, kind="ExternalInput").ap()
    wrd = nc.dram_tensor("wrt", [D, 8], F32, kind="ExternalInput").ap()
    bcd = nc.dram_tensor("bcat", [128, D], F32, kind="ExternalInput").ap()
    brd = nc.dram_tensor("br", [8, 1], F32, kind="ExternalInput").ap()
    e16d = nc.dram_tensor("e16", [8, 128], BF16, kind="ExternalInput").ap()
    idd = nc.dram_tensor("ident", [128, 128], BF16, kind="ExternalInput").ap()
    idfd = nc.dram_tensor("identf", [128, 128], F32, kind="ExternalInput").ap()
    yd = nc.dram_tensor("y", [D, MC], F32, kind="ExternalOutput").ap()
    NLOAD = 8
    with ExitStack() as es:
        sb = lambda n, s, t: es.enter_context(nc.sbuf_tensor(n, s, t))
        psa = lambda n, s, t: es.enter_context(nc.psum_tensor(n, s, t))
        xts = sb("xts", [128, nk * MC], BF16)
        acf = sb("acf", [128, nk * 128], F32)
        acb = sb("acb", [128, nk * 128], BF16)
        wrf = sb("wrf", [128, nk * 8], F32)
        wrb = sb("wrb", [128, nk * 8], BF16)
        bcf = sb("bcf", [128, D], F32)
        bcb = sb("bcb", [128, D], BF16)
        brs = sb("brs", [8, 1], F32)
        e16s = sb("e16s", [8, 128], BF16)
        idt = sb("idt", [128, 128], BF16)
        idf = sb("idf", [128, 128], F32)
        low = sb("low", [128, MC], BF16)
        lgt = sb("lgt", [8, MC], F32)
        ltm = sb("ltm", [128, 8], F32)
        nmr = sb("nmr", [128, 1], F32)
        denr = sb("denr", [128, 1], F32)
        rcpr = sb("rcpr", [128, 1], F32)
        enr = sb("enr", [128, 8], F32)
        efm = sb("efm", [8, MC], BF16)
        lows = sb("lows", [128, 512], BF16)
        yo = sb("yo", [128, MC], F32)
        lpp = [psa(f"lpp{i}", [128, 512], F32) for i in range(2)]
        rpp = [psa(f"rpp{i}", [8, 512], F32) for i in range(2)]
        tpp = [psa(f"tpp{i}", [128, 128], F32) for i in range(2)]
        epp = psa("epp", [128, 512], F32)
        opp = psa("opp", [128, 512], F32)
        sems = {}
        for name in ("ld", "SM", "SV", "SA", "SS0", "SS1"):
            sems[name] = es.enter_context(nc.semaphore(name))
        block = es.enter_context(nc.Block())
        A = AutoSched()

        A.add("vector", lambda v: v.tensor_copy(acb[:, :], acf[:, :]), reads=[], writes=["acb"])
        A.add("vector", lambda v: v.tensor_copy(wrb[:, :], wrf[:, :]), writes=["wrb"])
        A.add("vector", lambda v: v.tensor_copy(bcb[:, :], bcf[:, :]), writes=["bcb"])
        tpi = {"i": 0}
        for m in range(2):
            mb = m % 2
            for k in range(nk):
                A.add("tensor", lambda t, k=k, m=m, mb=mb: t.matmul(
                    lpp[mb][:, :], acb[:, 128 * k:128 * (k + 1)], xts[:, MC * k + 512 * m:MC * k + 512 * (m + 1)],
                    start=(k == 0), stop=(k == nk - 1)),
                    reads=["acb"], writes=[("lpp", mb)])
                A.add("tensor", lambda t, k=k, m=m, mb=mb: t.matmul(
                    rpp[mb][:, :], wrb[:, 8 * k:8 * (k + 1)], xts[:, MC * k + 512 * m:MC * k + 512 * (m + 1)],
                    start=(k == 0), stop=(k == nk - 1)),
                    reads=["wrb"], writes=[("rpp", mb)])
            A.add("vector", lambda v, m=m, mb=mb: v.tensor_copy(low[:, 512 * m:512 * (m + 1)], lpp[mb][:, :]),
                  reads=[("lpp", mb)], writes=[("low", m)])
            A.add("vector", lambda v, m=m, mb=mb: v.tensor_scalar(lgt[:, 512 * m:512 * (m + 1)], rpp[mb][:, :], 1.0, brs[0:8, 0:1], ALU.mult, ALU.add),
                  reads=[("rpp", mb)], writes=[("lgt", m)])
            for tt in range(4):
                col = 512 * m + 128 * tt
                i = tpi["i"]; tpi["i"] += 1
                pb = i % 2
                A.add("tensor", lambda t, col=col, pb=pb: t.transpose(tpp[pb][:, 0:8], lgt[:, col:col + 128], idf[0:8, 0:8]),
                      reads=[("lgt", m)], writes=[("tpp", pb)])
                A.add("vector", lambda v, pb=pb: v.tensor_copy(ltm[:, :], tpp[pb][:, 0:8]),
                      reads=[("tpp", pb)], writes=["ltm"])
                A.add("vector", lambda v: v.reduce_max(nmr[:, :], ltm[:, :], axis=AX.X, negate=True),
                      reads=["ltm"], writes=["nmr"])
                A.add("scalar", lambda a: a.activation(enr[:, :], ltm[:, :], ACT.Exp, bias=nmr[:, 0:1], accum_out=denr[:, :]),
                      reads=["ltm", "nmr"], writes=["enr", "denr"])
                A.add("vector", lambda v: v.reciprocal(rcpr[:, :], denr[:, :]),
                      reads=["denr"], writes=["rcpr"])
                A.add("vector", lambda v: v.tensor_scalar(enr[:, :], enr[:, :], rcpr[:, 0:1], None, ALU.mult),
                      reads=["enr", "rcpr"], writes=["enr"])
                i2 = tpi["i"]; tpi["i"] += 1
                pb2 = i2 % 2
                A.add("tensor", lambda t, pb2=pb2: t.transpose(tpp[pb2][0:8, :], enr[:, :], idf[:, :]),
                      reads=["enr"], writes=[("tpp", pb2)])
                A.add("vector", lambda v, col=col, pb2=pb2: v.tensor_copy(efm[:, col:col + 128], tpp[pb2][0:8, :]),
                      reads=[("tpp", pb2)], writes=[("efm", m, tt)])
            A.add("tensor", lambda t, m=m: t.matmul(epp[:, :], e16s[:, :], efm[:, 512 * m:512 * (m + 1)], start=True, stop=True),
                  reads=[("efm", m, 0), ("efm", m, 1), ("efm", m, 2), ("efm", m, 3)], writes=["epp"])
            A.add("vector", lambda v, m=m: v.tensor_tensor(lows[:, :], low[:, 512 * m:512 * (m + 1)], epp[:, :], ALU.mult),
                  reads=[("low", m), "epp"], writes=["lows"])
            for dm in range(nk):
                A.add("tensor", lambda t, dm=dm: t.matmul(opp[:, :], bcb[:, 128 * dm:128 * (dm + 1)], lows[:, :], start=True, stop=True),
                      reads=["bcb", "lows"], writes=["opp"])
                A.add("vector", lambda v, dm=dm, m=m: v.tensor_copy(yo[:, 512 * m:512 * (m + 1)], opp[:, :]),
                      reads=["opp"], writes=[("yo", m)])
                A.add("scalar", lambda a, dm=dm, m=m: a.dma_start(
                    out=yd[128 * dm:128 * (dm + 1), 512 * m:512 * (m + 1)], in_=yo[:, 512 * m:512 * (m + 1)]),
                    reads=[("yo", m)], sem=f"SS{m}", amount=16)
        A.plan()

        @block.sync
        def _(sync):
            sync.dma_start(out=xts.ap().rearrange("p (kc mm) -> p kc mm", kc=nk),
                           in_=xtd.rearrange("(kc p) mm -> p kc mm", p=128)).then_inc(sems["ld"], 16)
            sync.dma_start(out=acf.ap().rearrange("p (kc n) -> p kc n", kc=nk),
                           in_=acd.rearrange("(kc p) n -> p kc n", p=128)).then_inc(sems["ld"], 16)
            sync.dma_start(out=wrf.ap().rearrange("p (kc n) -> p kc n", kc=nk),
                           in_=wrd.rearrange("(kc p) n -> p kc n", p=128)).then_inc(sems["ld"], 16)
            sync.dma_start(out=bcf[:, :], in_=bcd[:, :]).then_inc(sems["ld"], 16)
            sync.dma_start(out=brs[:, :], in_=brd[:, :]).then_inc(sems["ld"], 16)
            sync.dma_start(out=e16s[:, :], in_=e16d[:, :]).then_inc(sems["ld"], 16)
            sync.dma_start(out=idt[:, :], in_=idd[:, :]).then_inc(sems["ld"], 16)
            sync.dma_start(out=idf[:, :], in_=idfd[:, :]).then_inc(sems["ld"], 16)
            sync.wait_ge(sems["SS0"], 16 * nk)
            sync.wait_ge(sems["SS1"], 16 * nk)

        @block.vector
        def _(vector):
            vector.wait_ge(sems["ld"], 16 * NLOAD)
            A.emit("vector", vector, sems)

        @block.scalar
        def _(scalar):
            scalar.wait_ge(sems["ld"], 16 * NLOAD)
            A.emit("scalar", scalar, sems)

        @block.tensor
        def _(tensor):
            tensor.wait_ge(sems["ld"], 16 * NLOAD)
            A.emit("tensor", tensor, sems)
    return nc


def launch_attn1(q1, k1, v1t, consts):
    nc = get_nc("attn1", build_attn1)
    cos, sin = rope_tables(1.0)
    cosq, sinq = rope_tables(HD ** -0.5)
    fm = lambda tab: np.ascontiguousarray(np.concatenate([tab.T, tab.T], 0).astype(np.float32))
    ins = []
    for c in range(NCORES):
        b, hh = c // 2, c % 2
        vsl = np.ascontiguousarray(v1t[b * S:(b + 1) * S, hh * 4 * HD:(hh + 1) * 4 * HD])
        v2 = np.zeros_like(vsl)
        v2[64:] = vsl[:-64]
        ins.append({
            "qf": np.ascontiguousarray(q1[hh * 16 * HD:(hh + 1) * 16 * HD, b * S:(b + 1) * S]),
            "kf": np.ascontiguousarray(k1[hh * 4 * HD:(hh + 1) * 4 * HD, b * S:(b + 1) * S]),
            "vt": vsl, "vt2": v2,
            "ident": consts["ident"], "rotmT": consts["rotmT"], "mask01": consts["mask01"],
            "madd_swa": consts["madd_swa"], "madd_swa0": consts["madd_swa0"],
            "cosq_fm": fm(cosq), "sinq_fm": fm(sinq), "cosk_fm": fm(cos), "sink_fm": fm(sin),
        })
    outs = run(nc, ins)
    o = np.zeros((NH * HD, MTOK), dtype=_BF)
    for c in range(NCORES):
        b, hh = c // 2, c % 2
        o[hh * 16 * HD:(hh + 1) * 16 * HD, b * S:(b + 1) * S] = outs[c]["o"]
    return o


def launch_ffn1(xT, Acat, WrT, Bcat, br, consts):
    nc = get_nc("ffn1", build_ffn1)
    ins = [{"xt": np.ascontiguousarray(xT[:, MC * c:MC * (c + 1)]),
            "acat": Acat, "wrt": WrT, "bcat": Bcat, "br": br,
            "e16": consts["e16"], "ident": consts["ident"],
            "identf": np.eye(128, dtype=np.float32)} for c in range(NCORES)]
    outs = run(nc, ins)
    return np.concatenate([o["y"] for o in outs], axis=1)



def kernel(L0_attn_hidden_states, L0_attn_Wq, L0_attn_Wk, L0_attn_Wv, L0_attn_Wo,
           L0_ffn_gate_qweight, L0_ffn_gate_scales, L0_ffn_gate_zeros,
           L0_ffn_up_qweight, L0_ffn_up_scales, L0_ffn_up_zeros,
           L0_ffn_down_qweight, L0_ffn_down_scales, L0_ffn_down_zeros,
           L1_attn_Wq, L1_attn_Wk, L1_attn_Wv, L1_attn_Wo,
           L1_ffn_A, L1_ffn_B, L1_ffn_Wr, L1_ffn_br):
    consts = const_tiles()
    x0 = np.ascontiguousarray(np.asarray(L0_attn_hidden_states, np.float32).reshape(MTOK, D))
    cos, sin = rope_tables(1.0)
    cosq, sinq = rope_tables(HD ** -0.5)

    ln0 = launch_ln(x0)
    xT = np.ascontiguousarray(ln0.T[PERM_D, :])
    qT = launch_gemm(xT, np.ascontiguousarray(np.asarray(L0_attn_Wq, np.float32).T[PERM_D, :]))
    kT = launch_gemm(xT, np.ascontiguousarray(np.asarray(L0_attn_Wk, np.float32).T[PERM_D, :]))
    vT = launch_gemm(xT, np.ascontiguousarray(np.asarray(L0_attn_Wv, np.float32).T[PERM_D, :]))
    qr, kr, vb = launch_rope(np.ascontiguousarray(qT.T), np.ascontiguousarray(kT.T),
                             np.ascontiguousarray(vT.T), cosq, sinq, cos, sin)
    o0 = launch_attn0(qr, kr, vb, consts)
    aoT = launch_gemm(np.ascontiguousarray(o0.T), np.ascontiguousarray(np.asarray(L0_attn_Wo, np.float32).T))
    x1 = launch_add(x0, np.ascontiguousarray(aoT.T))

    Wg = launch_dequant(np.asarray(L0_ffn_gate_qweight), np.asarray(L0_ffn_gate_scales),
                        np.asarray(L0_ffn_gate_zeros), 256)
    Wu = launch_dequant(np.asarray(L0_ffn_up_qweight), np.asarray(L0_ffn_up_scales),
                        np.asarray(L0_ffn_up_zeros), 256)
    Wd = launch_dequant(np.asarray(L0_ffn_down_qweight), np.asarray(L0_ffn_down_scales),
                        np.asarray(L0_ffn_down_zeros), 0)
    ln1 = launch_ln(x1)
    xT1 = np.ascontiguousarray(ln1.T[PERM_D, :])
    gT = launch_gemm(xT1, Wg, out_dt=BF16)
    uT = launch_gemm(xT1, Wu, out_dt=BF16)
    inter = launch_silumul(gT, uT)
    f0T = launch_gemm_bigk(np.ascontiguousarray(inter[PERM_I, :]), Wd)
    x2 = launch_add(x1, np.ascontiguousarray(f0T.T))

    ln2 = launch_ln(x2)
    xT2 = np.ascontiguousarray(ln2.T[PERM_D, :])
    q1 = launch_gemm(xT2, np.ascontiguousarray(np.asarray(L1_attn_Wq, np.float32)[PERM_D, :]), out_dt=BF16)
    k1 = launch_gemm(xT2, np.ascontiguousarray(np.asarray(L1_attn_Wk, np.float32)[PERM_D, :]), out_dt=BF16)
    v1 = launch_gemm(xT2, np.ascontiguousarray(np.asarray(L1_attn_Wv, np.float32)[PERM_D, :]), out_dt=BF16)
    o1 = launch_attn1(q1, k1, np.ascontiguousarray(v1.T), consts)
    ao1T = launch_gemm(o1, np.ascontiguousarray(np.asarray(L1_attn_Wo, np.float32)))
    x3 = launch_add(x2, np.ascontiguousarray(ao1T.T))

    ln3 = launch_ln(x3)
    xT3 = np.ascontiguousarray(ln3.T[PERM_D, :])
    A_ = np.asarray(L1_ffn_A, np.float32)
    B_ = np.asarray(L1_ffn_B, np.float32)
    Wr_ = np.asarray(L1_ffn_Wr, np.float32)
    br_ = np.asarray(L1_ffn_br, np.float32)
    Acat = np.ascontiguousarray(A_.transpose(2, 0, 1).reshape(D, 128)[PERM_D, :])
    WrT = np.ascontiguousarray(Wr_.T[PERM_D, :])
    Bcat = np.ascontiguousarray(B_.transpose(0, 2, 1).reshape(128, D))
    f1T = launch_ffn1(xT3, Acat, WrT, Bcat, np.ascontiguousarray(br_.reshape(8, 1)), consts)
    out = launch_add(x3, np.ascontiguousarray(f1T.T))
    return out.reshape(B, S, D)
